# version 02
# baseline (speedup 1.0000x reference)
import numpy as np
import ml_dtypes
from contextlib import ExitStack
import concourse.bass as bass
import concourse.mybir as mybir
from concourse.bass_utils import run_bass_kernel_spmd

F32 = mybir.dt.float32
BF16 = mybir.dt.bfloat16
AF = mybir.ActivationFunctionType
ALU = mybir.AluOpType
AX = mybir.AxisListType

P = 128
NT = 16
D = 1024
DC = 8
N = 2048
FH = 8
FDH = 64
W1P = 22
HIDC = 22
AH = 16
ADH = 64
NEG = -1.0e9

DBG = {}
BUILD = {"ffn1_layers": 2, "attn": True, "ffn2_layers": 2}


def _rotary_tables():
    half = np.arange(0, FDH, 2, dtype=np.float64)
    inv_freq = 1.0 / 10000.0 ** (half / FDH)
    t = np.arange(N, dtype=np.float64)
    freqs = (t[:, None].astype(np.float32) * inv_freq[None, :].astype(np.float32)).astype(np.float64)
    freqs = np.concatenate([freqs, freqs], axis=-1)
    base = ((half.astype(np.float32) + 0.4 * FDH) / (1.4 * FDH)).astype(np.float64)
    power = ((t.astype(np.float32) - N // 2) / 256.0).astype(np.float64)
    scale = base[None, :] ** power[:, None]
    scale = np.concatenate([scale, scale], axis=-1)
    cos, sin = np.cos(freqs), np.sin(freqs)
    CQ = (cos * scale).astype(np.float32)
    SQ = (sin * scale).astype(np.float32)
    CK = (cos / scale).astype(np.float32)
    SK = (sin / scale).astype(np.float32)
    tile = lambda a: a.reshape(NT, P, FDH).transpose(1, 0, 2).copy()
    return tile(CQ), tile(SQ), tile(CK), tile(SK)


def pack_shared(ins):
    w = {}
    for l in range(2):
        w[f"f{l}_wqkv"] = np.ascontiguousarray(ins["L0_ffn_wqkv"][l].reshape(DC, P, 1536))
        w[f"f{l}_wo"] = np.ascontiguousarray(ins["L0_ffn_wo"][l].reshape(4, P, D))
        w1 = ins["L0_ffn_w1"][l]
        a = np.zeros((D, W1P * P), np.float32); a[:, :2730] = w1[:, :2730]
        g = np.zeros((D, W1P * P), np.float32); g[:, :2730] = w1[:, 2730:]
        w1p = np.zeros((W1P, DC, P, 256), np.float32)
        for j in range(W1P):
            w1p[j, :, :, 0:128] = a[:, P * j:P * j + P].reshape(DC, P, P)
            w1p[j, :, :, 128:256] = g[:, P * j:P * j + P].reshape(DC, P, P)
        w[f"f{l}_w1"] = w1p
        b1 = ins["L0_ffn_b1"][l]
        b1a = np.zeros((W1P * P,), np.float32); b1a[:2730] = b1[:2730]
        b1g = np.zeros((W1P * P,), np.float32); b1g[:2730] = b1[2730:]
        w[f"f{l}_b1a"] = np.ascontiguousarray(b1a.reshape(W1P, P).T)
        w[f"f{l}_b1g"] = np.ascontiguousarray(b1g.reshape(W1P, P).T)
        w2p = np.zeros((HIDC, P, D), np.float32)
        w2p.reshape(HIDC * P, D)[:2730] = ins["L0_ffn_w2"][l]
        w[f"f{l}_w2"] = w2p
        w[f"f{l}_b2"] = np.ascontiguousarray(ins["L0_ffn_b2"][l].reshape(1, D))
        w[f"f{l}_lng"] = np.ascontiguousarray(ins["L0_ffn_ln_g"][l].reshape(DC, P).T)
        w[f"f{l}_lnb"] = np.ascontiguousarray(ins["L0_ffn_ln_b"][l].reshape(DC, P).T)
        w[f"f{l}_qs"] = np.ascontiguousarray(ins["L0_ffn_q_scale"][l].reshape(1, FDH))
        w[f"f{l}_ks"] = np.ascontiguousarray(ins["L0_ffn_k_scale"][l].reshape(1, FDH))
    Wqkv = ins["L0_attn_W_qkv"]
    w["a_wq"] = np.ascontiguousarray(Wqkv[:, 0:D].reshape(DC, P, D))
    w["a_wkv"] = np.ascontiguousarray(Wqkv[:, D:].reshape(DC, P, 2 * D))
    w["a_wout"] = np.ascontiguousarray(ins["L0_attn_W_out"].reshape(DC, P, D))
    w["a_bout"] = np.ascontiguousarray(ins["L0_attn_b_out"].reshape(1, D))
    CQ, SQ, CK, SK = _rotary_tables()
    w["tab_cq"], w["tab_sq"], w["tab_ck"], w["tab_sk"] = CQ, SQ, CK, SK
    return w


class Em:
    ENGS = ("pe", "act", "dve", "pool", "sync")

    def __init__(self, nc, ctx):
        self.nc = nc
        self.ops = {e: [] for e in self.ENGS}
        self.sems = {}
        self.counts = {}
        self.waited = {e: {} for e in self.ENGS}
        for s in ("pe", "act", "dve", "pool",
                  "dW0", "dW1", "dT", "dX", "dS",
                  "dB", "dBW"):
            self.sems[s] = ctx.enter_context(nc.semaphore(s))
            self.counts[s] = 0

    def raw(self, eng, fn):
        self.ops[eng].append(fn)

    def _emit(self, eng, fn, inc=None, dinc=None):
        tok = None
        if inc is not None:
            self.counts[inc] += 1
            tok = (inc, self.counts[inc])
            sem = self.sems[inc]
            self.ops[eng].append(lambda e, fn=fn, sem=sem: fn(e).then_inc(sem, 1))
        elif dinc is not None:
            prev = self.counts[dinc]
            if prev > 0:
                self.wait(eng, (dinc, prev))
            self.counts[dinc] += 16
            tok = (dinc, self.counts[dinc])
            sem = self.sems[dinc]
            self.ops[eng].append(lambda e, fn=fn, sem=sem: fn(e).then_inc(sem, 16))
        else:
            self.ops[eng].append(fn)
        return tok

    def pe(self, fn, inc=None):   return self._emit("pe", fn, inc)
    def act(self, fn, inc=None):  return self._emit("act", fn, inc)
    def dve(self, fn, inc=None):  return self._emit("dve", fn, inc)
    def pool(self, fn, inc=None, dinc=None): return self._emit("pool", fn, inc, dinc)
    def sync(self, fn, dinc=None): return self._emit("sync", fn, None, dinc)

    def wait(self, eng, *tokens):
        best = {}
        for t in tokens:
            if t is None:
                continue
            name, val = t
            if val > best.get(name, -1):
                best[name] = val
        for name, val in best.items():
            if self.waited[eng].get(name, -1) >= val:
                continue
            self.waited[eng][name] = val
            sem = self.sems[name]
            self.ops[eng].append(lambda e, sem=sem, val=val: e.wait_ge(sem, val))

    def drain(self, eng):
        self.ops[eng].append(lambda e: e.drain())

    def tok(self, name):
        return (name, self.counts[name])

    def replay(self, block):
        em = self

        @block.sync
        def _(e):
            for f in em.ops["sync"]: f(e)

        @block.tensor
        def _(e):
            for f in em.ops["pe"]: f(e)

        @block.scalar
        def _(e):
            for f in em.ops["act"]: f(e)

        @block.vector
        def _(e):
            for f in em.ops["dve"]: f(e)

        @block.gpsimd
        def _(e):
            for f in em.ops["pool"]: f(e)


def build_program(dbg=(), build=None):
    bld = dict(BUILD)
    if build: bld.update(build)
    nc = bass.Bass("TRN2", target_bir_lowering=False)
    ctx = ExitStack()

    dx = nc.dram_tensor("x", [N, D], F32, kind="ExternalInput").ap()
    dout = nc.dram_tensor("out", [N, D], F32, kind="ExternalOutput").ap()
    dw = {}
    for l in range(2):
        dw[f"f{l}_wqkv"] = nc.dram_tensor(f"f{l}_wqkv", [DC, P, 1536], F32, kind="ExternalInput").ap()
        dw[f"f{l}_wo"] = nc.dram_tensor(f"f{l}_wo", [4, P, D], F32, kind="ExternalInput").ap()
        dw[f"f{l}_w1"] = nc.dram_tensor(f"f{l}_w1", [W1P, DC, P, 256], F32, kind="ExternalInput").ap()
        dw[f"f{l}_b1a"] = nc.dram_tensor(f"f{l}_b1a", [P, W1P], F32, kind="ExternalInput").ap()
        dw[f"f{l}_b1g"] = nc.dram_tensor(f"f{l}_b1g", [P, W1P], F32, kind="ExternalInput").ap()
        dw[f"f{l}_w2"] = nc.dram_tensor(f"f{l}_w2", [HIDC, P, D], F32, kind="ExternalInput").ap()
        dw[f"f{l}_b2"] = nc.dram_tensor(f"f{l}_b2", [1, D], F32, kind="ExternalInput").ap()
        dw[f"f{l}_lng"] = nc.dram_tensor(f"f{l}_lng", [P, DC], F32, kind="ExternalInput").ap()
        dw[f"f{l}_lnb"] = nc.dram_tensor(f"f{l}_lnb", [P, DC], F32, kind="ExternalInput").ap()
        dw[f"f{l}_qs"] = nc.dram_tensor(f"f{l}_qs", [1, FDH], F32, kind="ExternalInput").ap()
        dw[f"f{l}_ks"] = nc.dram_tensor(f"f{l}_ks", [1, FDH], F32, kind="ExternalInput").ap()
    for nm, shp in (("a_wq", [DC, P, D]), ("a_wkv", [DC, P, 2 * D]),
                    ("a_wout", [DC, P, D]), ("a_bout", [1, D]),
                    ("tab_cq", [P, NT, FDH]), ("tab_sq", [P, NT, FDH]),
                    ("tab_ck", [P, NT, FDH]), ("tab_sk", [P, NT, FDH])):
        dw[nm] = nc.dram_tensor(nm, shp, F32, kind="ExternalInput").ap()
    dspill = nc.dram_tensor("spill", [N, D], F32, kind="Internal").ap()
    dbounce = nc.dram_tensor("bounce", [4, N + 2], F32, kind="Internal").ap()
    ddbg = {}
    for nm, shp, dt in dbg:
        ddbg[nm] = nc.dram_tensor(nm, shp, dt, kind="ExternalOutput").ap()

    sb = lambda nm, shp, dt: ctx.enter_context(nc.sbuf_tensor(nm, shp, dt)).ap()
    xbuf = sb("xbuf", [P, NT, D], F32)
    ybuf = sb("ybuf", [P, NT, D], F32)
    aren = sb("arena", [P, 13312], F32)
    tabs = sb("tabs", [P, 4, NT, FDH], F32)
    stat = sb("stat", [P, 416], F32)
    consts = sb("consts", [P, 392], F32)
    identb = sb("identb", [P, P], BF16)
    identf = sb("identf", [P, P], F32)
    onesf = sb("onesf", [1, P], F32)
    onesbf = sb("onesbf", [P, 2], BF16)
    onesrow_bf = sb("onesrow_bf", [1, P], BF16)
    bT32 = sb("bT32", [64, P], F32)
    fcst = sb("fcst", [P, 8], F32)
    bT32b = sb("bT32b", [32, P], F32)
    smallrow = sb("smallrow", [1, 1024], F32)
    ps = lambda nm: ctx.enter_context(nc.psum_tensor(nm, [P, 1024], F32)).ap()
    pa, pb, pc, pd = ps("pa"), ps("pb"), ps("pc"), ps("pd")
    pbf = {id(pa): pa.bitcast(BF16), id(pb): pb.bitcast(BF16),
           id(pc): pc.bitcast(BF16), id(pd): pd.bitcast(BF16)}

    em = Em(nc, ctx)

    FCONSTS = [1e-5, 1e-6, 0.0, 1.0, -1.0, 1e-12, 8.0]
    for _i, _v in enumerate(FCONSTS):
        nc.const_aps.register(F32, _v, fcst[:, _i:_i + 1])

    XA = xbuf.rearrange("p n d -> p (n d)")
    YA = ybuf.rearrange("p n d -> p (n d)")

    def carve(region, off_w, shape, dt):
        n = int(np.prod(shape))
        if dt == F32:
            v = region[:, off_w:off_w + n]
        else:
            v = region[:, off_w:off_w + (n + 1) // 2].bitcast(BF16)[:, 0:n]
        if len(shape) > 1:
            dims = " ".join(f"d{i}" for i in range(len(shape)))
            v = v.rearrange(f"p ({dims}) -> p {dims}",
                            **{f"d{i}": s for i, s in enumerate(shape)})
        return v

    ysT = carve(XA, 0, (DC, 2056), BF16)
    wqkv_bf = carve(XA, 8224, (DC, 1536), BF16)
    w1p_bf = [carve(XA, 8224 + 1024 * i, (DC, 256), BF16) for i in range(2)]
    w2c_bf = [carve(XA, 10272 + 512 * i, (D,), BF16) for i in range(2)]
    b1a_c = carve(XA, 14336, (W1P,), F32)
    b1g_c = carve(XA, 14358, (W1P,), F32)
    lng_c = carve(XA, 14380, (DC,), F32)
    lnb_c = carve(XA, 14388, (DC,), F32)
    b2row = smallrow[0:1, 0:D]
    QT = carve(aren, 0, (4, N), BF16)
    KT = carve(aren, 4096, (4, N), BF16)
    Vb = carve(aren, 8192, (NT, 512), BF16)
    h3b = carve(aren, 0, (HIDC, 1024), BF16)
    scrA = carve(aren, 12288, (1024,), F32)
    scrB = carve(XA, 15268, (1024,), F32)
    scr1 = [scrA, scrB]
    Esb = [carve(aren, 12288 + 384 * i, (768,), BF16) for i in range(2)]
    gelu_s = [carve(aren, 12288 + 512 * i, (1024,), BF16) for i in range(2)]
    z0tok = carve(YA, 0, (NT, D), BF16)
    z0T = carve(XA, 0, (DC, N), BF16)
    qhT = carve(YA, 8192, (DC, N), BF16)
    wq_bf = [carve(XA, 8192 + 512 * i, (DC, P), BF16) for i in range(2)]
    wkv_bf2 = carve(XA, 8192, (DC, 2 * D), BF16)
    wout_bf2 = carve(XA, 8192, (DC, D), BF16)
    kt_scr = carve(aren, 0, (1024,), BF16)
    kvt = carve(aren, 512, (1040,), BF16)
    elu_f = carve(aren, 1056, (1024,), F32)
    kv_sb = carve(aren, 2080, (640,), F32)
    KVr = carve(aren, 2720, (DC, 130), BF16)
    Dtmp = carve(aren, 3240, (16,), F32)
    Drcp = carve(aren, 3272, (16,), F32)
    OTt = [carve(aren, 3584 + 1024 * i, (DC, P), BF16) for i in range(2)]
    xld = [carve(aren, 5632 + 1024 * i, (1024,), F32) for i in range(2)]
    xtmp = [carve(aren, 7680 + 1024 * i, (1024,), F32) for i in range(2)]
    otile = [carve(aren, 9728 + 512 * i, (1024,), BF16) for i in range(2)]

    _stoff = [0]
    def stcol(n):
        o = _stoff[0]; _stoff[0] += n
        assert _stoff[0] <= 416
        return stat[:, o:o + n]
    st_s1 = stcol(16); st_s2 = stcol(16); st_q1 = stcol(16); st_q2 = stcol(16)
    st_mu = stcol(16); st_rs = stcol(16); st_nmu = stcol(16); st_tmp = stcol(16)
    st_mush = stcol(16); st_rssh = stcol(16)
    st_ngm = stcol(2); st_den = stcol(2); st_rden = stcol(2)
    qs_rep = stcol(64); qs_rrep = stcol(64)
    st_rl2 = stcol(16)
    ks_rep = consts[:, 256:320]
    ks_rrep = consts[:, 320:384]
    maskL = consts[:, 0:128]
    maskU = consts[:, 128:256]

    def bcast(ap_col, free):
        return ap_col.unsqueeze(-1).broadcast_to(list(ap_col.shape) + [free])

    def emit_init():
        em.pool(lambda e: e.memset(identb, 0.0))
        em.pool(lambda e: e.memset(identf, 0.0))
        em.pool(lambda e: e.memset(maskL, 0.0))
        em.pool(lambda e: e.memset(maskU, 0.0))
        em.pool(lambda e: e.memset(onesf, 1.0))
        em.pool(lambda e: e.memset(onesbf, 1.0))
        em.pool(lambda e: e.memset(onesrow_bf, 1.0))
        em.pool(lambda e: e.memset(stat[:, 0:416], 0.0))
        for _i, _v in enumerate(FCONSTS):
            em.pool(lambda e, i=_i, v=_v: e.memset(fcst[:, i:i + 1], v))
        em.drain("pool")
        em.pool(lambda e: e.affine_select(out=identb, in_=identb, compare_op=ALU.not_equal,
                                          fill=1.0, base=0, pattern=[[-1, P]], channel_multiplier=1))
        em.pool(lambda e: e.affine_select(out=identf, in_=identf, compare_op=ALU.not_equal,
                                          fill=1.0, base=0, pattern=[[-1, P]], channel_multiplier=1))
        em.pool(lambda e: e.affine_select(out=maskL, in_=maskL, compare_op=ALU.is_ge,
                                          fill=NEG, base=0, pattern=[[-1, P]], channel_multiplier=1))
        t = em.pool(lambda e: e.affine_select(out=maskU, in_=maskU, compare_op=ALU.is_ge,
                                              fill=NEG, base=0, pattern=[[1, P]], channel_multiplier=-1),
                    inc="pool")
        em.pool(lambda e: e.dma_start(dbounce[0:4, 0:1], fcst[0:4, 2:3], allow_slow_non_contiguous=True), dinc="dB")
        em.pool(lambda e: e.dma_start(dbounce[0:4, 2049:2050], fcst[0:4, 2:3], allow_slow_non_contiguous=True), dinc="dB")
        return t

    def pe_transpose(src_bf, dst_psum_bf, ncols=P):
        em.raw("pe", lambda e, s=src_bf, d=dst_psum_bf: e.transpose(d, s, identb[0:ncols, 0:ncols] if False else identb))

    def emit_load_x():
        t = em.sync(lambda e: e.dma_start(xbuf, dx.rearrange("(n p) d -> p n d", p=P)), dinc="dX")
        return t

    def emit_ln_outer(dst_tok, ready_tok):
        sq_read = None
        for n in range(NT):
            xt = xbuf[:, n, :]
            em.wait("dve", ready_tok)
            em.dve(lambda e, xt=xt, n=n: e.reduce_sum(st_s1[:, n:n + 1], xt, axis=AX.X))
            em.wait("act", ready_tok, sq_read)
            t_sq = em.act(lambda e, xt=xt: e.square(scrA, xt), inc="act")
            em.wait("dve", t_sq)
            sq_read = em.dve(
                lambda e, n=n: e.reduce_sum(st_q1[:, n:n + 1], scrA, axis=AX.X), inc="dve")
        em.drain("dve")
        em.dve(lambda e: e.tensor_scalar(st_mu, st_s1, 1.0 / D, None, ALU.mult))
        em.drain("dve")
        em.dve(lambda e: e.tensor_mul(st_tmp, st_mu, st_mu))
        em.drain("dve")
        em.dve(lambda e: e.tensor_scalar(st_rs, st_q1, 1.0 / D, None, ALU.mult))
        em.drain("dve")
        t0 = em.dve(lambda e: e.tensor_sub(st_rs, st_rs, st_tmp), inc="dve")
        em.wait("act", t0)
        t1 = em.act(lambda e: e.activation(st_rs, st_rs, AF.Sqrt, bias=1e-5), inc="act")
        em.wait("dve", t1)
        em.dve(lambda e: e.reciprocal(st_rs, st_rs))
        em.dve(lambda e: e.tensor_scalar(st_nmu, st_mu, -1.0, None, ALU.mult))
        em.drain("dve")
        out_toks = []
        for n in range(NT):
            xt = xbuf[:, n, :]
            dt_ = dst_tok[:, n, :]
            tt = em.dve(lambda e, xt=xt, dt_=dt_, n=n: e.tensor_scalar(
                dt_, xt, st_nmu[:, n:n + 1], st_rs[:, n:n + 1], ALU.add, ALU.mult), inc="dve")
            out_toks.append(tt)
        return out_toks

    castbf = carve(XA, 14500, (1024,), BF16)
    qrotb = carve(XA, 15012, (512,), BF16)
    pd_bf = pbf[id(pd)]
    pa_bf = pbf[id(pa)]
    pb_bf = pbf[id(pb)]
    pc_bf = pbf[id(pc)]

    def emit_ffn_tabprep(l, prev_tok=None):
        em.wait("sync", prev_tok, em.tok("dB"))
        tq = em.sync(lambda e: e.dma_start(smallrow[0:1, 0:FDH], dw[f"f{l}_qs"]), dinc="dT")
        tk = em.sync(lambda e: e.dma_start(smallrow[0:1, 64:64 + FDH], dw[f"f{l}_ks"]), dinc="dT")
        ttabs = [em.sync(lambda e, nm=nm, i=i: e.dma_start(tabs[:, i], dw[nm]), dinc="dT")
                 for i, nm in enumerate(("tab_cq", "tab_sq", "tab_ck", "tab_sk"))]
        em.wait("pe", tq, tk)
        t0 = em.pe(lambda e: e.matmul(pa[:, 0:64], onesf, smallrow[0:1, 0:64],
                                      start=True, stop=True), inc="pe")
        t1 = em.pe(lambda e: e.matmul(pa[:, 64:128], onesf, smallrow[0:1, 64:128],
                                      start=True, stop=True), inc="pe")
        em.wait("act", t0, t1)
        ta = em.act(lambda e: e.activation(qs_rep, pa[:, 0:64], AF.Copy, scale=8.0), inc="act")
        tb = em.act(lambda e: e.copy(ks_rep, pa[:, 64:128]), inc="act")
        em.wait("dve", ta, tb)
        em.dve(lambda e: e.tensor_scalar(qs_rrep[:, 0:32], qs_rep[:, 32:64], -1.0, None, ALU.mult))
        em.dve(lambda e: e.tensor_copy(qs_rrep[:, 32:64], qs_rep[:, 0:32]))
        em.dve(lambda e: e.tensor_scalar(ks_rrep[:, 0:32], ks_rep[:, 32:64], -1.0, None, ALU.mult))
        em.dve(lambda e: e.tensor_copy(ks_rrep[:, 32:64], ks_rep[:, 0:32]))
        em.drain("dve")
        em.wait("dve", *ttabs)
        reps = (qs_rep, qs_rrep, ks_rep, ks_rrep)
        tt = None
        for i in range(4):
            tt = em.dve(lambda e, i=i, r=reps[i]: e.tensor_tensor(
                tabs[:, i], tabs[:, i],
                r.unsqueeze(1).broadcast_to([P, NT, FDH]), ALU.mult), inc="dve")
        em.drain("dve")
        return tt

    def emit_ffn_qkv(l, y_toks, prev_attn_done=None):
        em.wait("pool", prev_attn_done)
        t_w = em.pool(lambda e: e.dma_start(
            wqkv_bf, dw[f"f{l}_wqkv"].rearrange("c p w -> p c w")), dinc="dBW")
        em.wait("dve", prev_attn_done)
        t_z = em.dve(lambda e: e.memset(ysT[:, :, 0:1], 0.0), inc="dve")
        t_tab = emit_ffn_tabprep(l, prev_attn_done)

        t_ys_evac = [None] * NT
        t_mm = [None] * NT
        t_cast = [None] * NT
        t_tr = [None] * NT
        t_qk_evac = [None] * NT
        t_v_evac = [None] * NT
        t_qt = [None] * NT
        t_sqr = [None] * NT
        t_red = [None] * NT
        t_rotq = [None] * NT
        t_rotk = [None] * NT
        for n in range(NT):
            par = n % 2
            em.wait("pool", y_toks[n], t_tr[n - 1] if n >= 1 else None)
            t_cast[n] = em.pool(lambda e, n=n: e.tensor_copy(castbf, ybuf[:, n, :]), inc="pool")
            em.wait("pe", t_cast[n], t_ys_evac[n - 2] if n >= 2 else None)
            for c in range(DC):
                tk = em.pe(lambda e, n=n, c=c, par=par: e.transpose(
                    pd_bf[:, 1024 * par + 128 * c:1024 * par + 128 * c + 128],
                    castbf[:, 128 * c:128 * c + 128], identb), inc="pe" if c == DC - 1 else None)
            t_tr[n] = tk
            em.wait("dve", t_tr[n], t_z)
            t_ys_evac[n] = em.dve(lambda e, n=n, par=par: e.tensor_copy(
                ysT[:, :, 1 + 128 * n:1 + 128 * n + 128],
                pd_bf[:, 1024 * par:1024 * par + 1024].rearrange("p (c t) -> p c t", c=DC)),
                inc="dve")
        em.wait("pe", t_w, t_tab)
        for n in range(NT):
            par = n % 2
            pqk = pa if par == 0 else pc
            pqk_bf = pa_bf if par == 0 else pc_bf
            pv = pb[:, 0:512] if par == 0 else pb[:, 512:1024]
            em.wait("pe", t_ys_evac[n],
                    t_qk_evac[n - 2] if n >= 2 else None,
                    t_v_evac[n - 2] if n >= 2 else None)
            for dc in range(DC):
                off = (1 + 128 * n) if dc < 4 else (128 * n)
                st = dc == 0
                ys_sl = ysT[:, dc, off:off + 128]
                em.pe(lambda e, ys_sl=ys_sl, pqk=pqk, dc=dc, st=st:
                      e.matmul(pqk[:, 0:512], ys_sl, wqkv_bf[:, dc, 0:512], start=st, stop=dc == DC - 1))
                em.pe(lambda e, ys_sl=ys_sl, pqk=pqk, dc=dc, st=st:
                      e.matmul(pqk[:, 512:1024], ys_sl, wqkv_bf[:, dc, 512:1024], start=st, stop=dc == DC - 1))
                t_mm[n] = em.pe(lambda e, ys_sl=ys_sl, pv=pv, dc=dc, st=st:
                                e.matmul(pv, ys_sl, wqkv_bf[:, dc, 1024:1536], start=st,
                                         stop=dc == DC - 1), inc="pe" if dc == DC - 1 else None)
            em.wait("act", t_mm[n], t_rotk[n - 1] if n >= 1 else None)
            t_sqr[n] = em.act(lambda e, pqk=pqk: e.square(scrB, pqk[:, 0:1024]), inc="act")
            em.wait("dve", t_sqr[n], t_qk_evac[n - 1] if n >= 1 else None)
            t_red[n] = em.dve(lambda e: e.reduce_sum(
                st_rl2, scrB.rearrange("p (h d) -> p h d", h=16), axis=AX.X), inc="dve")
            em.wait("act", t_red[n])
            t_sq2 = em.act(lambda e: e.activation(st_tmp, st_rl2, AF.Sqrt), inc="act")
            em.wait("dve", t_sq2)
            em.dve(lambda e: e.tensor_scalar(st_tmp, st_tmp, 1e-12, None, ALU.max))
            em.drain("dve")
            em.dve(lambda e: e.reciprocal(st_rl2, st_tmp))
            em.drain("dve")
            t_qk_evac[n] = em.dve(lambda e, pqk=pqk: e.tensor_tensor(
                scrA.rearrange("p (h d) -> p h d", h=16),
                pqk[:, 0:1024].rearrange("p (h d) -> p h d", h=16),
                st_rl2.unsqueeze(2).broadcast_to([P, 16, FDH]), ALU.mult), inc="dve")
            em.wait("act", t_mm[n], t_qt[n - 2] if n >= 2 else None)
            t_v_evac[n] = em.act(lambda e, pv=pv, n=n: e.copy(Vb[:, n, :], pv), inc="act")
            qv = scrA.rearrange("p (h d) -> p h d", h=16)
            rt = scrB.rearrange("p (x d) -> p x d", x=16)
            t_prev_evac = t_qt[n - 1] if n >= 1 else None
            tp_prev = None
            for qk in range(2):
                tab_c = tabs[:, 0 if qk == 0 else 2, n, :]
                tab_s = tabs[:, 1 if qk == 0 else 3, n, :]
                hv = qv[:, 8 * qk:8 * qk + 8]
                em.drain("dve")
                em.dve(lambda e, hv=hv, tab_c=tab_c: e.tensor_tensor(
                    rt[:, 0:8], hv, tab_c.unsqueeze(1).broadcast_to([P, 8, FDH]), ALU.mult))
                em.dve(lambda e, hv=hv, tab_s=tab_s: e.tensor_tensor(
                    rt[:, 8:16, 0:32], hv[:, :, 32:64],
                    tab_s[:, 0:32].unsqueeze(1).broadcast_to([P, 8, 32]), ALU.mult))
                em.dve(lambda e, hv=hv, tab_s=tab_s: e.tensor_tensor(
                    rt[:, 8:16, 32:64], hv[:, :, 0:32],
                    tab_s[:, 32:64].unsqueeze(1).broadcast_to([P, 8, 32]), ALU.mult))
                em.drain("dve")
                em.wait("dve", tp_prev)
                tr_ = em.dve(lambda e: e.tensor_add(
                    qrotb, scrB[0:P, 0:512], scrB[0:P, 512:1024]), inc="dve")
                if qk == 0:
                    t_rotq[n] = tr_
                else:
                    t_rotk[n] = tr_
                DST = QT if qk == 0 else KT
                em.wait("pe", tr_, t_prev_evac,
                        t_ys_evac[14 + par] if n <= 1 else None)
                for j in range(4):
                    tp = em.pe(lambda e, j=j, par=par: e.transpose(
                        pd_bf[:, 1024 * par + 128 * j:1024 * par + 128 * j + 128],
                        qrotb[:, 128 * j:128 * j + 128], identb),
                        inc="pe" if j == 3 else None)
                tp_prev = tp
                em.wait("dve", tp)
                t_prev_evac = em.dve(lambda e, DST=DST, n=n, par=par: e.tensor_copy(
                    DST[:, :, 128 * n:128 * n + 128],
                    pd_bf[:, 1024 * par:1024 * par + 512].rearrange("p (c t) -> p c t", c=4)),
                    inc="dve")
            t_qt[n] = t_prev_evac
        return dict(qt=t_qt[NT - 1], v=t_v_evac, mm=t_mm, last_dve=em.tok("dve"),
                    last_act=em.tok("act"), last_pe=em.tok("pe"))

    ETbs = [carve(XA, 0, (NT * 640,), BF16), carve(XA, 5120, (NT * 640,), BF16)]
    OT = carve(XA, 10240, (4, N), BF16)
    wo_bf = carve(XA, 14336, (4, D), BF16)

    def emit_ffn_attn(l, qkv_r):
        em.wait("pool", qkv_r["last_pe"])
        t_wo = em.pool(lambda e: e.dma_start(
            wo_bf, dw[f"f{l}_wo"].rearrange("c p w -> p c w")), dinc="dBW")
        t_exp = {}
        t_tr = {}
        t_ev = {}
        t_av_ev = {}
        t_msk = {}
        t_rmax = {}
        last_av_mm = {}
        qt_ready = qkv_r["qt"]
        for hp in range(4):
            for hi in range(2):
                h = 2 * hp + hi
                ET = ETbs[hi]
                qrow = 64 * (h % 2)
                for qt in range(NT):
                    par = qt % 2
                    pS = pa if par == 0 else pb
                    k0 = max(0, 128 * qt - 512)
                    w = 128 * qt + 128 - k0
                    kc0 = k0 // 128
                    nch = w // 128
                    em.wait("pe", qt_ready,
                            t_exp.get((h, qt - 2)), t_exp.get((h - 1, qt + NT - 2)))
                    lhs = QT[qrow:qrow + 64, h // 2, 128 * qt:128 * qt + 128]
                    if w <= 512:
                        t_s = em.pe(lambda e, lhs=lhs, pS=pS, k0=k0, w=w, h=h, qrow=qrow: e.matmul(
                            pS[:, 0:w], lhs, KT[qrow:qrow + 64, h // 2, k0:k0 + w],
                            start=True, stop=True), inc="pe")
                    else:
                        em.pe(lambda e, lhs=lhs, pS=pS, k0=k0, h=h, qrow=qrow: e.matmul(
                            pS[:, 0:512], lhs, KT[qrow:qrow + 64, h // 2, k0:k0 + 512],
                            start=True, stop=True))
                        t_s = em.pe(lambda e, lhs=lhs, pS=pS, k0=k0, w=w, h=h, qrow=qrow: e.matmul(
                            pS[:, 512:w], lhs, KT[qrow:qrow + 64, h // 2, k0 + 512:k0 + w],
                            start=True, stop=True), inc="pe")
                    em.wait("dve", t_s)
                    em.dve(lambda e, pS=pS, w=w: e.tensor_add(
                        pS[:, w - 128:w], pS[:, w - 128:w], maskL))
                    if k0 > 0:
                        em.dve(lambda e, pS=pS: e.tensor_add(pS[:, 0:128], pS[:, 0:128], maskU))
                    em.drain("dve")
                    t_msk[(h, qt)] = em.dve(lambda e, pS=pS, w=w, par=par: e.reduce_max(
                        st_ngm[:, par:par + 1], pS[:, 0:w], axis=AX.X, negate=True), inc="dve")
                    em.wait("act", t_msk[(h, qt)], t_tr.get((h, qt - 2)), t_tr.get((h - 1, qt + NT - 2)))
                    t_exp[(h, qt)] = em.act(lambda e, pS=pS, w=w, par=par: e.activation(
                        Esb[par][0:P, 0:w], pS[:, 0:w], AF.Exp,
                        bias=st_ngm[:, par:par + 1], accum_out=st_den[:, par:par + 1]), inc="act")
                    em.wait("dve", t_exp[(h, qt)])
                    em.dve(lambda e, par=par: e.reciprocal(
                        st_rden[:, par:par + 1], st_den[:, par:par + 1]))
                    em.drain("dve")
                    t_nrm = em.dve(lambda e, w=w, par=par: e.tensor_scalar(
                        Esb[par][0:P, 0:w], Esb[par][0:P, 0:w],
                        st_rden[:, par:par + 1], None, ALU.mult), inc="dve")
                    em.wait("pe", t_nrm, t_ev.get((h, qt - 2)), t_ev.get((h - 1, qt + NT - 2)))
                    pcoff = 1024 * par
                    for c in range(nch):
                        t_t = em.pe(lambda e, c=c, par=par, pcoff=pcoff: e.transpose(
                            pc_bf[:, pcoff + 128 * c:pcoff + 128 * c + 128],
                            Esb[par][0:P, 128 * c:128 * c + 128], identb),
                            inc="pe" if c == nch - 1 else None)
                    t_tr[(h, qt)] = t_t
                    em.wait("dve", t_t)
                    base = 512 * kc0 + 128 * qt
                    dst = ET[:, base:base + 512 * nch].rearrange(
                        "p (c j) -> p c j", j=512)[:, :, 0:128] if nch > 1 else \
                        ET[:, base:base + 128].unsqueeze(1)
                    t_ev[(h, qt)] = em.dve(lambda e, dst=dst, par=par, pcoff=pcoff, nch=nch:
                                           e.tensor_copy(dst, pc_bf[:, pcoff:pcoff + 128 * nch]
                                                         .rearrange("p (c j) -> p c j", c=nch)),
                                           inc="dve")
            for H in range(2):
                em.wait("pe", t_ev[(2 * hp, NT - 1)], t_ev[(2 * hp + 1, NT - 1)],
                        t_av_ev.get((hp, H - 1)), t_av_ev.get((hp - 1, H + 1)))
                t_avmm = None
                bank_first = {}
                for hi in range(2):
                    ET = ETbs[hi]
                    h = 2 * hp + hi
                    for kc in range(max(0, 8 * H - 4), min(NT, 8 * H + 8)):
                        a = max(128 * kc, 1024 * H)
                        b = min(128 * kc + 640, 1024 * H + 1024, N)
                        ft = 0 if kc == 0 else 128 * kc + 512
                        frs = []
                        x0 = a
                        while x0 < b:
                            x1 = min(b, (x0 // 512 + 1) * 512)
                            if x0 < ft < x1:
                                x1 = ft
                            frs.append((x0, x1))
                            x0 = x1
                        for (x0, x1) in frs:
                            bk = (hi, (x0 - 1024 * H) // 512)
                            st = not bank_first.get(bk, False)
                            bank_first[bk] = True
                            t_avmm = em.pe(lambda e, hi=hi, h=h, kc=kc, x0=x0, x1=x1, H=H, ET=ET, st=st:
                                           e.matmul(pd[64 * hi:64 * hi + 64, x0 - 1024 * H:x1 - 1024 * H],
                                                    Vb[:, kc, 64 * h:64 * h + 64],
                                                    ET[:, 640 * kc + x0 - 128 * kc:640 * kc + x1 - 128 * kc],
                                                    start=st, stop=True, skip_group_check=True,
                                                    tile_position=(0, 64) if hi else None),
                                           inc="pe")
                last_av_mm[(hp, H)] = t_avmm
                em.wait("act", t_avmm)
                t_av_ev[(hp, H)] = em.act(lambda e, hp=hp, H=H: e.copy(
                    OT[:, hp, 1024 * H:1024 * H + 1024], pd[:, 0:1024]), inc="act")
        return dict(ot=[t_av_ev[(hp, H)] for hp in range(4) for H in range(2)],
                    wo_tok=t_wo, last_pe=em.tok("pe"), last_dve=em.tok("dve"),
                    last_act=em.tok("act"))

    def emit_ffn_wo(l, attn_r):
        ot_toks = attn_r["ot"]
        t_res = [None] * NT
        t_mm = [None] * NT
        for tt in range(NT):
            par = tt % 2
            pw = pa if par == 0 else pb
            em.wait("pe", attn_r["wo_tok"], *ot_toks)
            em.wait("pe", t_res[tt - 2] if tt >= 2 else None)
            for ic in range(4):
                em.pe(lambda e, tt=tt, ic=ic, pw=pw: e.matmul(
                    pw[:, 0:512], OT[:, ic, 128 * tt:128 * tt + 128],
                    wo_bf[:, ic, 0:512], start=ic == 0, stop=ic == 3))
                t_mm[tt] = em.pe(lambda e, tt=tt, ic=ic, pw=pw: e.matmul(
                    pw[:, 512:1024], OT[:, ic, 128 * tt:128 * tt + 128],
                    wo_bf[:, ic, 512:1024], start=ic == 0, stop=ic == 3),
                    inc="pe" if ic == 3 else None)
            em.wait("dve", t_mm[tt])
            t_res[tt] = em.dve(lambda e, tt=tt, pw=pw: e.tensor_add(
                ybuf[:, tt, :], ybuf[:, tt, :], pw[:, 0:1024]), inc="dve")
        return dict(y=t_res, last_pe=em.tok("pe"), last_dve=em.tok("dve"))

    def emit_ffn_mlp(l, wo_r):
        y_toks = wo_r["y"]
        em.wait("sync", wo_r["last_pe"])
        t_ld = [em.sync(lambda e, nm=nm, dst=dst: e.dma_start(dst, dw[nm]), dinc="dT")
                for nm, dst in ((f"f{l}_lng", lng_c), (f"f{l}_lnb", lnb_c),
                                (f"f{l}_b1a", b1a_c), (f"f{l}_b1g", b1g_c))]
        b2rowb = smallrow.bitcast(BF16)[0:1, 0:D]
        em.wait("pool", wo_r["last_pe"])
        t_b2 = em.pool(lambda e: e.dma_start(b2rowb, dw[f"f{l}_b2"]), dinc="dB")
        s12 = stat[:, 0:32]
        q12 = stat[:, 32:64]
        t_red2 = [None] * NT
        sq_tok = [None]
        for tt in range(NT):
            par = tt % 2
            sq = scrA
            em.wait("dve", y_toks[tt])
            em.dve(lambda e, tt=tt: e.reduce_sum(
                bass.AP(s12.tensor, s12.offset + tt, [list(s12.ap[0]), [16, 2]]),
                ybuf[:, tt, :].rearrange("p (h x) -> p h x", h=2), axis=AX.X))
            em.wait("act", y_toks[tt], sq_tok[0])
            t_sq = em.act(lambda e, tt=tt, sq=sq: e.square(sq, ybuf[:, tt, :]), inc="act")
            em.wait("dve", t_sq)
            sq_tok[0] = em.dve(lambda e, tt=tt, sq=sq: e.reduce_sum(
                bass.AP(q12.tensor, q12.offset + tt, [list(q12.ap[0]), [16, 2]]),
                sq.rearrange("p (h x) -> p h x", h=2), axis=AX.X), inc="dve")
            t_red2[tt] = sq_tok[0]
        em.drain("dve")
        t_all = em.dve(lambda e: e.engine_nop(), inc="dve") if False else t_red2[NT - 1]
        em.wait("pe", *t_red2)
        tr1 = em.pe(lambda e: e.transpose(pa[0:64, 0:128], stat[:, 0:64], identf), inc="pe")
        em.wait("act", tr1)
        tc1 = em.act(lambda e: e.copy(bT32, pa[0:64, 0:128]), inc="act")
        em.wait("pool", tc1)
        em.pool(lambda e: e.dma_start(
            dbounce[0, 1:2049].rearrange("(t p) -> t p", p=P), bT32[16:32, :]), dinc="dB")
        tb1 = em.pool(lambda e: e.dma_start(
            dbounce[1, 1:2049].rearrange("(t p) -> t p", p=P), bT32[48:64, :]), dinc="dB")
        tb2 = em.pool(lambda e: e.dma_start(
            bT32b[:, :],
            dbounce[0:2, 0:2048].rearrange("r (t p) -> r t p", p=P)), dinc="dB")
        em.wait("pe", tb2)
        tr2 = em.pe(lambda e: e.transpose(pa[:, 128:160], bT32b, identf[0:32, 0:32]), inc="pe")
        em.wait("act", tr2)
        tc2 = em.act(lambda e: e.copy(stat[:, 64:96], pa[:, 128:160]), inc="act")
        s2sh = stat[:, 64:80]; q2sh = stat[:, 80:96]
        nmu = stat[:, 96:112]; rr = stat[:, 112:128]
        tmp1 = stat[:, 128:144]; tmp2 = stat[:, 144:160]
        em.wait("dve", tc2)
        em.dve(lambda e: e.tensor_add(tmp1, stat[:, 0:16], s2sh))
        em.drain("dve")
        em.dve(lambda e: e.tensor_scalar(nmu, tmp1, -1.0 / D, None, ALU.mult))
        em.drain("dve")
        em.dve(lambda e: e.tensor_add(tmp1, stat[:, 32:48], q2sh))
        em.drain("dve")
        em.dve(lambda e: e.tensor_scalar(tmp1, tmp1, 1.0 / D, None, ALU.mult))
        em.dve(lambda e: e.tensor_mul(tmp2, nmu, nmu))
        em.drain("dve")
        t_var = em.dve(lambda e: e.tensor_sub(tmp1, tmp1, tmp2), inc="dve")
        em.wait("act", t_var)
        t_sq3 = em.act(lambda e: e.activation(tmp1, tmp1, AF.Sqrt, bias=1e-5), inc="act")
        em.wait("dve", t_sq3)
        t_rr = em.dve(lambda e: e.reciprocal(rr, tmp1), inc="dve")
        em.wait("pe", t_rr)
        tr3 = em.pe(lambda e: e.transpose(pa[0:32, 0:128], stat[:, 96:128], identf), inc="pe")
        em.wait("act", tr3)
        tc3 = em.act(lambda e: e.copy(bT32[0:32, :], pa[0:32, 0:128]), inc="act")
        em.wait("pool", tc3)
        tb3 = em.pool(lambda e: e.dma_start(
            dbounce[2:4, 1:2049].rearrange("r (t p) -> r t p", p=P),
            bT32[0:32, :]), dinc="dB")
        tb4 = em.pool(lambda e: e.dma_start(
            bT32b[:, :],
            dbounce[2:4, 2:2050].rearrange("r (t p) -> r t p", p=P)), dinc="dB")
        em.wait("pe", tb4)
        tr4 = em.pe(lambda e: e.transpose(pa[:, 128:160], bT32b, identf[0:32, 0:32]), inc="pe")
        em.wait("act", tr4)
        tc4 = em.act(lambda e: e.copy(stat[:, 160:192], pa[:, 128:160]), inc="act")
        nmush = stat[:, 160:176]; rrsh = stat[:, 176:192]
        em.wait("dve", t_rr, *t_ld)
        em.dve(lambda e: e.tensor_mul(stat[0:1, 192:193], nmu[0:1, 0:1], rr[0:1, 0:1]))
        em.drain("dve")
        t_z0 = em.dve(lambda e: e.tensor_copy(smallrow[0:1, 600:601], stat[0:1, 192:193]), inc="dve")
        em.wait("pe", t_z0)
        t_z1 = em.pe(lambda e: e.matmul(pa[:, 160:161], onesf, smallrow[0:1, 600:601],
                                        start=True, stop=True), inc="pe")
        em.wait("act", t_z1)
        t_z2 = em.act(lambda e: e.copy(stat[:, 193:194], pa[:, 160:161]), inc="act")
        em.wait("dve", t_z2, tc4)
        t_zc = em.dve(lambda e: e.scalar_tensor_tensor(
            ysT[:, 4:8, 0:1], lng_c[:, 4:8].unsqueeze(2), stat[:, 193:194],
            lnb_c[:, 4:8].unsqueeze(2), ALU.mult, ALU.add), inc="dve")
        t_ht = [None] * NT
        t_tr7 = [None] * NT
        em.wait("dve", wo_r["last_pe"])
        for tt in range(NT):
            par = tt % 2
            em.wait("dve", t_tr7[tt - 1] if tt >= 1 else None)
            em.dve(lambda e, tt=tt: e.tensor_scalar(
                castbf[:, 0:512], ybuf[:, tt, 0:512],
                nmu[:, tt:tt + 1], rr[:, tt:tt + 1], ALU.add, ALU.mult))
            t_ap = em.dve(lambda e, tt=tt: e.tensor_scalar(
                castbf[:, 512:1024], ybuf[:, tt, 512:1024],
                nmush[:, tt:tt + 1], rrsh[:, tt:tt + 1], ALU.add, ALU.mult), inc="dve")
            em.wait("pe", t_ap, t_ht[tt - 2] if tt >= 2 else None)
            for c in range(DC):
                t_tr7[tt] = em.pe(lambda e, c=c, par=par: e.transpose(
                    pd_bf[:, 1024 * par + 128 * c:1024 * par + 128 * c + 128],
                    castbf[:, 128 * c:128 * c + 128], identb),
                    inc="pe" if c == DC - 1 else None)
            em.wait("dve", t_tr7[tt], t_zc)
            for c in range(DC):
                t_ht[tt] = em.dve(lambda e, tt=tt, c=c, par=par: e.tensor_scalar(
                    ysT[:, c, 1 + 128 * tt:1 + 128 * tt + 128],
                    pd_bf[:, 1024 * par + 128 * c:1024 * par + 128 * c + 128],
                    lng_c[:, c:c + 1], lnb_c[:, c:c + 1], ALU.mult, ALU.add),
                    inc="dve" if c == DC - 1 else None)
        t_res = []
        t_w1p = [None, None]
        t_stt = {}
        t_gel = {}
        t_w2mm = {}
        t_w2c = [None, None]
        t_radd = {}
        for Hq in range(2):
            for pr in range(W1P):
                par = pr % 2
                em.wait("pool", wo_r["last_pe"],
                        t_stt.get((Hq, pr - 2)), t_stt.get((Hq - 1, pr + W1P - 2)))
                t_w1p[par] = em.pool(lambda e, pr=pr, par=par: e.dma_start(
                    w1p_bf[par], dw[f"f{l}_w1"][pr].rearrange("c p w -> p c w")), dinc="dBW")
                psA = pa if par == 0 else pc
                psB = pb if par == 0 else pd
                em.wait("pe", t_w1p[par], t_ht[NT - 1],
                        t_stt.get((Hq, pr - 2)), t_stt.get((Hq - 1, pr + W1P - 2)),
                        t_gel.get((Hq, pr - 2)), t_gel.get((Hq - 1, pr + W1P - 2)),
                        *[t_radd.get((Hq - 1, 1, i)) for i in range(4)])
                for dc in range(DC):
                    off = (1 + 1024 * Hq) if dc < 4 else (1024 * Hq)
                    for half2 in range(2):
                        em.pe(lambda e, pr=pr, dc=dc, off=off, h2=half2, psA=psA, par=par:
                              e.matmul(psA[:, 512 * h2:512 * h2 + 512],
                                       w1p_bf[par][:, dc, 0:128],
                                       ysT[:, dc, off + 512 * h2:off + 512 * h2 + 512],
                                       start=dc == 0, stop=dc == DC - 1))
                        t_mmB = em.pe(lambda e, pr=pr, dc=dc, off=off, h2=half2, psB=psB, par=par:
                                      e.matmul(psB[:, 512 * h2:512 * h2 + 512],
                                               w1p_bf[par][:, dc, 128:256],
                                               ysT[:, dc, off + 512 * h2:off + 512 * h2 + 512],
                                               start=dc == 0, stop=dc == DC - 1),
                                      inc="pe" if (dc == DC - 1 and half2 == 1) else None)
                em.wait("act", t_mmB, t_stt.get((Hq, pr - 2)), t_stt.get((Hq - 1, pr + W1P - 2)))
                t_gel[(Hq, pr)] = em.act(lambda e, pr=pr, par=par, psB=psB: e.activation(
                    gelu_s[par], psB[:, 0:1024], AF.Gelu, bias=b1g_c[:, pr:pr + 1]), inc="act")
                em.wait("dve", t_gel[(Hq, pr)])
                t_stt[(Hq, pr)] = em.dve(lambda e, pr=pr, par=par, psA=psA: e.scalar_tensor_tensor(
                    h3b[:, pr, :], psA[:, 0:1024], b1a_c[:, pr:pr + 1], gelu_s[par],
                    ALU.add, ALU.mult), inc="dve")
            for qr in range(2):
                pw2 = [pa, pb, pc, pd]
                for hc in range(HIDC):
                    par = hc % 2
                    em.wait("pool", t_w2mm.get((Hq, qr, hc - 2)),
                            t_w2mm.get((Hq, qr - 1, hc + HIDC - 2)),
                            t_w2mm.get((Hq - 1, qr + 1, hc + HIDC - 2)))
                    t_w2c[par] = em.pool(lambda e, hc=hc, par=par: e.dma_start(
                        w2c_bf[par], dw[f"f{l}_w2"][hc]), dinc="dB")
                    em.wait("pe", t_w2c[par], t_stt[(Hq, W1P - 1)],
                            *[t_radd.get((Hq, qr - 1, i)) for i in range(4)],
                            *[t_radd.get((Hq - 1, qr + 1, i)) for i in range(4)])
                    for i in range(4):
                        tt2 = 8 * Hq + 4 * qr + i
                        for h2 in range(2):
                            t_w2mm[(Hq, qr, hc)] = em.pe(
                                lambda e, hc=hc, i=i, tt2=tt2, h2=h2, par=par, qr=qr:
                                e.matmul(pw2[i][:, 512 * h2:512 * h2 + 512],
                                         h3b[:, hc, 128 * (4 * qr + i):128 * (4 * qr + i) + 128],
                                         w2c_bf[par][:, 512 * h2:512 * h2 + 512],
                                         start=hc == 0, stop=False),
                                inc="pe" if (i == 3 and h2 == 1) else None)
                em.wait("pe", t_b2)
                for i in range(4):
                    for h2 in range(2):
                        t_aug = em.pe(lambda e, i=i, h2=h2: e.matmul(
                            pw2[i][:, 512 * h2:512 * h2 + 512], onesrow_bf,
                            b2rowb[0:1, 512 * h2:512 * h2 + 512],
                            start=False, stop=True),
                            inc="pe" if (i == 3 and h2 == 1) else None)
                for i in range(4):
                    tt2 = 8 * Hq + 4 * qr + i
                    em.wait("dve", t_aug)
                    t_radd[(Hq, qr, i)] = em.dve(lambda e, i=i, tt2=tt2: e.tensor_add(
                        ybuf[:, tt2, :], ybuf[:, tt2, :], pw2[i][:, 0:1024]), inc="dve")
                    t_res.append(t_radd[(Hq, qr, i)])
        yt = [t_radd[(tt // 8, (tt % 8) // 4, tt % 4)] for tt in range(NT)]
        return dict(y=yt, last_pe=em.tok("pe"), last_dve=em.tok("dve"), last_act=em.tok("act"))

    def emit_ffn_layer(l, y_toks, prev_tok):
        r = emit_ffn_qkv(l, y_toks, prev_attn_done=prev_tok)
        ra = emit_ffn_attn(l, r)
        rw = emit_ffn_wo(l, ra)
        rm = emit_ffn_mlp(l, rw)
        return rm

    def emit_ffn_call(ln_apply_toks, prev_tok, nlayers=2, spill=True):
        em.wait("sync", *ln_apply_toks)
        if spill:
            t_sp = em.sync(lambda e: e.dma_start(
                dspill.rearrange("(n p) d -> p n d", p=P), xbuf), dinc="dS")
        else:
            t_sp = ln_apply_toks[-1]
        yt = ln_apply_toks
        prev = (t_sp, prev_tok)
        rm = None
        for li in range(nlayers):
            em.wait("dve", *prev)
            gate = em.dve(lambda e: e.engine_nop(), inc="dve")
            rm = emit_ffn_layer(li, yt, gate)
            yt = rm["y"]
            prev = (rm["last_pe"], rm["last_act"], rm["last_dve"])
        em.wait("sync", em.tok("dB"), em.tok("dBW"), *prev)
        t_rx = em.sync(lambda e: e.dma_start(
            xbuf, dspill.rearrange("(n p) d -> p n d", p=P)), dinc="dX")
        x_toks = []
        for n in range(NT):
            em.wait("dve", t_rx, yt[n])
            x_toks.append(em.dve(lambda e, n=n: e.scalar_tensor_tensor(
                xbuf[:, n, :], ybuf[:, n, :], 0.5, xbuf[:, n, :],
                ALU.mult, ALU.add), inc="dve"))
        return x_toks, rm

    def emit_lin_attn(x_toks, prev_tok):
        em.wait("dve", *x_toks)
        em.wait("act", *x_toks)
        ln_toks = emit_ln_outer(z0tok, x_toks[-1])
        em.wait("sync", *ln_toks)
        t_sp = em.sync(lambda e: e.dma_start(
            dspill.rearrange("(n p) d -> p n d", p=P), xbuf), dinc="dS")
        em.wait("dve", t_sp, prev_tok)
        t_z0 = [None] * NT
        t_tr = [None] * NT
        for n in range(NT):
            par = n % 2
            em.wait("pe", ln_toks[n], t_z0[n - 2] if n >= 2 else None, t_sp, prev_tok)
            for c in range(DC):
                t_tr[n] = em.pe(lambda e, n=n, c=c, par=par: e.transpose(
                    pd_bf[:, 1024 * par + 128 * c:1024 * par + 128 * c + 128],
                    z0tok[:, n, 128 * c:128 * c + 128], identb),
                    inc="pe" if c == DC - 1 else None)
            em.wait("dve", t_tr[n])
            t_z0[n] = em.dve(lambda e, n=n, par=par: e.tensor_copy(
                z0T[:, :, 128 * n:128 * n + 128],
                pd_bf[:, 1024 * par:1024 * par + 1024].rearrange("p (c t) -> p c t", c=DC)),
                inc="dve")
        em.wait("pool", prev_tok, t_sp)
        t_wq = [None, None]
        t_qe = {}
        for qf in range(DC):
            par = qf % 2
            em.wait("pool", t_qe.get((qf - 2, 3)))
            t_wq[par] = em.pool(lambda e, qf=qf, par=par: e.dma_start(
                wq_bf[par], dw["a_wq"].rearrange("c p w -> p c w")[:, :, 128 * qf:128 * qf + 128]),
                dinc="dBW")
            pq = [pa, pb] if par == 0 else [pc, pd]
            em.wait("pe", t_wq[par], t_z0[NT - 1], t_qe.get((qf - 2, 3)))
            for dc in range(DC):
                for tc in range(4):
                    t_qmm = em.pe(lambda e, qf=qf, dc=dc, tc=tc, pq=pq: e.matmul(
                        pq[tc // 2][:, 512 * (tc % 2):512 * (tc % 2) + 512],
                        wq_bf[qf % 2][:, dc, :], z0T[:, dc, 512 * tc:512 * tc + 512],
                        start=dc == 0, stop=dc == DC - 1),
                        inc="pe" if (dc == DC - 1 and tc == 3) else None)
            for tc in range(4):
                ps = pq[tc // 2][:, 512 * (tc % 2):512 * (tc % 2) + 512]
                em.wait("dve", t_qmm, t_qe.get((qf - 1, 3)))
                t_mn = em.dve(lambda e, ps=ps: e.tensor_scalar(
                    elu_f[:, 0:512], ps, 0.0, None, ALU.min), inc="dve")
                em.wait("act", t_mn)
                t_ex = em.act(lambda e: e.activation(elu_f[:, 512:1024], elu_f[:, 0:512], AF.Exp), inc="act")
                em.wait("dve", t_ex)
                t_qe[(qf, tc)] = em.dve(lambda e, ps=ps, qf=qf, tc=tc: e.scalar_tensor_tensor(
                    qhT[:, qf, 512 * tc:512 * tc + 512], ps, 0.0, elu_f[:, 512:1024],
                    ALU.max, ALU.add), inc="dve")
        em.wait("pool", t_qe[(DC - 1, 3)])
        t_wkv = em.pool(lambda e: e.dma_start(
            wkv_bf2, dw["a_wkv"].rearrange("c p w -> p c w")), dinc="dBW")
        em.pool(lambda e: e.memset(bass.AP(kvt.tensor, kvt.offset + 64, [list(kvt.ap[0]), [65, 16], [1, 1]]), 1.0))
        em.drain("pool")
        t_ones = em.pool(lambda e: e.engine_nop(), inc="pool")
        t_kv = {}
        t_ke = [None] * NT
        t_ve = [None] * NT
        for tt in range(NT):
            em.wait("pe", t_wkv, t_qe[(DC - 1, 3)],
                    t_kv.get(tt - 1), t_ke[tt - 1] if tt >= 1 else None)
            for dc in range(DC):
                for piece in range(4):
                    pko = (pa if piece < 2 else pb)
                    t_kvmm = em.pe(lambda e, tt=tt, dc=dc, piece=piece, pko=pko: e.matmul(
                        pko[:, 512 * (piece % 2):512 * (piece % 2) + 512],
                        z0T[:, dc, 128 * tt:128 * tt + 128],
                        wkv_bf2[:, dc, 512 * piece:512 * piece + 512],
                        start=dc == 0, stop=dc == DC - 1),
                        inc="pe" if (dc == DC - 1 and piece == 3) else None)
            em.wait("dve", t_kvmm)
            t_mn = em.dve(lambda e: e.tensor_scalar(
                elu_f, pa[:, 0:1024], 0.0, None, ALU.min), inc="dve")
            em.wait("act", t_mn)
            t_ex = em.act(lambda e: e.activation(scrA, elu_f, AF.Exp), inc="act")
            em.wait("dve", t_ex)
            t_ke[tt] = em.dve(lambda e: e.scalar_tensor_tensor(
                kt_scr, pa[:, 0:1024], 0.0, scrA, ALU.max, ALU.add), inc="dve")
            em.wait("act", t_kvmm, t_ones)
            t_ve[tt] = em.act(lambda e: e.copy(
                bass.AP(kvt.tensor, kvt.offset, [list(kvt.ap[0]), [65, 16], [1, 64]]),
                pb[:, 0:1024]), inc="act")
            em.wait("pe", t_ke[tt], t_ve[tt])
            for h in range(AH):
                p2 = h // 2
                col = 65 * p2 if p2 < 7 else 512
                st = (tt == 0) and (h in (0, 1, 14, 15))
                t_kv[tt] = em.pe(lambda e, h=h, col=col, st=st: e.matmul(
                    pd[64 * (h % 2):64 * (h % 2) + 64, col:col + 65],
                    kt_scr[:, 64 * h:64 * h + 64], kvt[:, 65 * h:65 * h + 65],
                    start=st, stop=True, skip_group_check=True,
                    tile_position=(0, 64) if h % 2 else None),
                    inc="pe" if h == AH - 1 else None)
        em.wait("act", t_kv[NT - 1])
        t_kve = em.act(lambda e: e.copy(kv_sb[:, 0:577], pd[:, 0:577]), inc="act")
        em.wait("pool", t_kve)
        em.pool(lambda e: e.memset(KVr, 0.0))
        em.drain("pool")
        t_kas = em.pool(lambda e: e.engine_nop(), inc="pool")
        em.wait("dve", t_kve, t_kas)
        t_as = None
        for c in range(DC):
            col = 65 * c if c < 7 else 512
            em.dve(lambda e, c=c, col=col: e.tensor_copy(
                KVr[0:64, c, 0:64], kv_sb[0:64, col:col + 64]))
            em.dve(lambda e, c=c, col=col: e.tensor_copy(
                KVr[64:128, c, 64:128], kv_sb[64:128, col:col + 64]))
            em.dve(lambda e, c=c, col=col: e.tensor_copy(
                KVr[0:64, c, 128:129], kv_sb[0:64, col + 64:col + 65]))
            t_as = em.dve(lambda e, c=c, col=col: e.tensor_copy(
                KVr[64:128, c, 129:130], kv_sb[64:128, col + 64:col + 65]),
                inc="dve" if c == DC - 1 else None)
        em.wait("pool", t_kv[NT - 1])
        t_wout = em.pool(lambda e: e.dma_start(
            wout_bf2, dw["a_wout"].rearrange("c p w -> p c w")), dinc="dBW")
        em.wait("pool", em.tok("dB"), prev_tok)
        boutb = smallrow.bitcast(BF16)[0:1, 0:D]
        t_bo = em.pool(lambda e: e.dma_start(boutb, dw["a_bout"]), dinc="dB")
        colof = lambda c: 512 * (c // 3) + 130 * (c % 3)
        t_num = [None] * NT
        t_div = {}
        t_otr = [None] * NT
        t_oev = [None] * NT
        t_wmm = [None] * NT
        t_add = [None] * NT
        t_xl = [None] * NT
        t_st = [None] * NT
        for tt in range(NT):
            par = tt % 2
            em.wait("sync", t_sp, t_add[tt - 2] if tt >= 2 else None)
            t_xl[tt] = em.sync(lambda e, tt=tt, par=par: e.dma_start(
                xld[par], dspill.rearrange("(n p) d -> p n d", p=P)[:, tt, :]), dinc="dX")
            em.wait("pe", t_as, t_div.get(tt - 1))
            for c in range(DC):
                pn = pa if c < 6 else pb
                co = colof(c) if c < 6 else colof(c) - 1024
                t_num[tt] = em.pe(lambda e, tt=tt, c=c, pn=pn, co=co: e.matmul(
                    pn[:, co:co + 130], qhT[:, c, 128 * tt:128 * tt + 128],
                    KVr[:, c, :], start=True, stop=True),
                    inc="pe" if c == DC - 1 else None)
            em.wait("dve", t_num[tt])
            em.dve(lambda e: e.tensor_copy(
                Dtmp[:, 0:12], bass.AP(pa.tensor, pa.offset + 128,
                                       [list(pa.ap[0]), [512, 2], [130, 3], [1, 2]])))
            t_dc = em.dve(lambda e: e.tensor_copy(
                Dtmp[:, 12:16], bass.AP(pb.tensor, pb.offset + 128,
                                        [list(pb.ap[0]), [130, 2], [1, 2]])), inc="dve")
            em.drain("dve")
            em.dve(lambda e: e.tensor_scalar(Dtmp, Dtmp, 1e-6, None, ALU.add))
            em.drain("dve")
            em.dve(lambda e: e.reciprocal(Drcp, Dtmp))
            em.drain("dve")
            for c in range(DC):
                pn = pa if c < 6 else pb
                co = colof(c) if c < 6 else colof(c) - 1024
                dslice = (Drcp[:, 2 * ((c // 3) * 3 + c % 3):2 * c + 2]
                          if True else None)
                t_div[tt] = em.dve(lambda e, tt=tt, c=c, pn=pn, co=co, par=par: e.tensor_tensor(
                    otile[par].rearrange("p (h d) -> p h d", h=16)[:, 2 * c:2 * c + 2, :],
                    pn[:, co:co + 128].rearrange("p (h d) -> p h d", h=2),
                    Drcp[:, 2 * c:2 * c + 2].unsqueeze(2).broadcast_to([P, 2, FDH]),
                    ALU.mult), inc="dve" if c == DC - 1 else None)
            em.wait("pe", t_div[tt], t_oev[tt - 2] if tt >= 2 else None, t_wout)
            for c in range(DC):
                t_otr[tt] = em.pe(lambda e, c=c, par=par: e.transpose(
                    pd_bf[:, 1024 * par + 128 * c:1024 * par + 128 * c + 128],
                    otile[par][:, 128 * c:128 * c + 128], identb),
                    inc="pe" if c == DC - 1 else None)
            em.wait("dve", t_otr[tt])
            t_oev[tt] = em.dve(lambda e, par=par: e.tensor_copy(
                OTt[par].rearrange("p c t -> p (c t)"),
                pd_bf[:, 1024 * par:1024 * par + 1024]), inc="dve")
            em.wait("pe", t_oev[tt], t_add[tt - 1] if tt >= 1 else None, t_bo)
            for ic in range(DC):
                for h2 in range(2):
                    t_wmm[tt] = em.pe(lambda e, ic=ic, h2=h2, par=par: e.matmul(
                        pc[:, 512 * h2:512 * h2 + 512], OTt[par][:, ic, :],
                        wout_bf2[:, ic, 512 * h2:512 * h2 + 512],
                        start=ic == 0, stop=False),
                        inc=None)
            for h2 in range(2):
                t_wmm[tt] = em.pe(lambda e, h2=h2: e.matmul(
                    pc[:, 512 * h2:512 * h2 + 512], onesrow_bf,
                    boutb[0:1, 512 * h2:512 * h2 + 512], start=False, stop=True),
                    inc="pe" if h2 == 1 else None)
            em.wait("dve", t_wmm[tt], t_xl[tt])
            t_add[tt] = em.dve(lambda e, par=par: e.tensor_add(
                xtmp[par], xld[par], pc[:, 0:1024]), inc="dve")
            em.wait("sync", t_add[tt])
            t_st[tt] = em.sync(lambda e, tt=tt, par=par: e.dma_start(
                dspill.rearrange("(n p) d -> p n d", p=P)[:, tt, :], xtmp[par]), dinc="dS")
        return t_st[NT - 1]

    def emit_dbg(nm, src_ap, after_tokens=()):
        if nm not in ddbg:
            return
        em.wait("sync", *after_tokens)
        em.sync(lambda e, s=src_ap, d=ddbg[nm]: e.dma_start(d, s), dinc="dS")

    def emit_out(src=None, after_tokens=()):
        em.wait("sync", *after_tokens)
        t = em.sync(lambda e: e.dma_start(dout.rearrange("(n p) d -> p n d", p=P),
                                          src if src is not None else xbuf), dinc="dS")
        em.wait("sync", t)
        em.wait("pe", t); em.wait("act", t); em.wait("dve", t); em.wait("pool", t)

    until = bld.get("until", "full")
    t_init = emit_init()
    t_x = emit_load_x()
    em.wait("dve", t_init)
    em.wait("act", t_init)
    em.wait("pe", t_init)
    ln_toks = emit_ln_outer(ybuf, t_x)
    emit_dbg("dbg_ln1", ybuf, ln_toks)
    if until == "ln1":
        emit_out(ybuf, tuple(ln_toks) + (t_init,))
    elif until == "qkv0":
        r = emit_ffn_qkv(0, ln_toks, prev_attn_done=ln_toks[-1])
        emit_dbg("dbg_qt", QT, (r["last_dve"],))
        emit_dbg("dbg_kt", KT, (r["last_dve"],))
        emit_dbg("dbg_v", Vb, (r["last_act"],))
        emit_out(ybuf, (r["last_dve"], r["last_act"], r["last_pe"]))
    elif until == "wo0":
        r = emit_ffn_qkv(0, ln_toks, prev_attn_done=ln_toks[-1])
        ra = emit_ffn_attn(0, r)
        emit_dbg("dbg_ot", OT, tuple(ra["ot"]))
        rw = emit_ffn_wo(0, ra)
        emit_out(ybuf, tuple(rw["y"]) + (rw["last_pe"],))
    elif until == "mlp0":
        r = emit_ffn_qkv(0, ln_toks, prev_attn_done=ln_toks[-1])
        ra = emit_ffn_attn(0, r)
        rw = emit_ffn_wo(0, ra)
        rm = emit_ffn_mlp(0, rw)
        emit_dbg("dbg_ht", ysT[:, :, 0:2049], (rm["last_dve"],))
        emit_dbg("dbg_stat", stat[:, 0:256], (rm["last_dve"],))
        emit_dbg("dbg_h3", h3b, (rm["last_dve"],))
        emit_out(ybuf, tuple(rm["y"]) + (rm["last_pe"], rm["last_act"]))
    elif until == "ffn1":
        x_toks, rm = emit_ffn_call(ln_toks, t_init)
        emit_out(None, tuple(x_toks))
    elif until == "attn":
        x_toks, rm = emit_ffn_call(ln_toks, t_init)
        gate = (rm["last_pe"], rm["last_act"], rm["last_dve"])
        em.wait("dve", *gate)
        g = em.dve(lambda e: e.engine_nop(), inc="dve")
        t_last = emit_lin_attn(x_toks, g)
        em.wait("sync", t_last)
        t_rx = em.sync(lambda e: e.dma_start(
            xbuf, dspill.rearrange("(n p) d -> p n d", p=P)), dinc="dX")
        emit_out(None, (t_rx,))
    elif until == "full":
        x_toks, rm = emit_ffn_call(ln_toks, t_init)
        gate = (rm["last_pe"], rm["last_act"], rm["last_dve"])
        em.wait("dve", *gate)
        g = em.dve(lambda e: e.engine_nop(), inc="dve")
        t_last = emit_lin_attn(x_toks, g)
        em.wait("sync", t_last, em.tok("dBW"), em.tok("dB"),
                em.tok("pe"), em.tok("act"), em.tok("dve"), em.tok("pool"))
        t_rx = em.sync(lambda e: e.dma_start(
            xbuf, dspill.rearrange("(n p) d -> p n d", p=P)), dinc="dX")
        em.wait("dve", t_rx)
        em.wait("act", t_rx)
        ln3 = emit_ln_outer(ybuf, t_rx)
        x_toks2, rm2 = emit_ffn_call(ln3, ln3[-1], spill=False)
        emit_out(None, tuple(x_toks2))

    with nc.Block() as block:
        em.replay(block)
    ctx.close()
    return nc


def kernel(L0_attn_x, L0_attn_W_qkv, L0_attn_W_out, L0_attn_b_out,
           L0_ffn_wqkv, L0_ffn_q_scale, L0_ffn_k_scale, L0_ffn_wo,
           L0_ffn_ln_g, L0_ffn_ln_b, L0_ffn_w1, L0_ffn_b1, L0_ffn_w2, L0_ffn_b2):
    ins = dict(L0_attn_x=np.asarray(L0_attn_x), L0_attn_W_qkv=np.asarray(L0_attn_W_qkv),
               L0_attn_W_out=np.asarray(L0_attn_W_out), L0_attn_b_out=np.asarray(L0_attn_b_out),
               L0_ffn_wqkv=np.asarray(L0_ffn_wqkv), L0_ffn_q_scale=np.asarray(L0_ffn_q_scale),
               L0_ffn_k_scale=np.asarray(L0_ffn_k_scale), L0_ffn_wo=np.asarray(L0_ffn_wo),
               L0_ffn_ln_g=np.asarray(L0_ffn_ln_g), L0_ffn_ln_b=np.asarray(L0_ffn_ln_b),
               L0_ffn_w1=np.asarray(L0_ffn_w1), L0_ffn_b1=np.asarray(L0_ffn_b1),
               L0_ffn_w2=np.asarray(L0_ffn_w2), L0_ffn_b2=np.asarray(L0_ffn_b2))
    shared = pack_shared(ins)
    nc = build_program()
    in_maps = []
    for b in range(8):
        m = {"x": np.ascontiguousarray(ins["L0_attn_x"][b])}
        m.update(shared)
        in_maps.append(m)
    res = run_bass_kernel_spmd(nc, in_maps, core_ids=list(range(8)))
    out = np.stack([r["out"] for r in res.results], axis=0)
    return out
